# version 10 (final)
# speedup vs baseline: 1.4948x; 1.0367x over previous
from contextlib import ExitStack

import numpy as np
import ml_dtypes

import concourse.bass as bass
import concourse.mybir as mybir
from concourse.bass_utils import run_bass_kernel_spmd
from concourse.libnrt import physical_to_logical_core_id

F32 = mybir.dt.float32
BF16 = mybir.dt.bfloat16
I32 = mybir.dt.int32
AF = mybir.ActivationFunctionType
ALU = mybir.AluOpType
AX = mybir.AxisListType
BFNP = ml_dtypes.bfloat16

DM = 2048
FC = 16
TC = 2048
T = 4096
INTER = 5632
NIC = 44
NG = 4
GIC = 11
HPC = 8
C = 128
NCHK = 32
RD = float(128.0 ** -0.5)


def _shard_of_arm(k):
    node4, phys = divmod(k, 128)
    return physical_to_logical_core_id(phys, node4, container_num_cores=8) % 8


class St:
    def __init__(self, nc, stack):
        self.nc = nc
        self.stack = stack
        self.sem_h = {}
        self.cnt = {}

    def sem(self, name):
        if name not in self.sem_h:
            self.sem_h[name] = self.stack.enter_context(self.nc.semaphore(name))
            self.cnt[name] = 0
        return self.sem_h[name]

    def inc(self, instr, name, k=1):
        instr.then_inc(self.sem(name), k)
        self.cnt[name] += k
        return self.cnt[name]

    def base(self, name):
        self.sem(name)
        return self.cnt[name]


BARS = ("t", "v", "s", "y", "g")
ENG_NAME = {"t": "tensor", "v": "vector", "s": "scalar", "y": "sync", "g": "gpsimd"}


class Phase:

    def __init__(self, st, name):
        self.S = st
        self.snap = {e: st.base(f"bar_{e}") for e in BARS}
        self.blk = None

    def __enter__(self):
        self.blk = self.S.nc.Block()
        self.blk.__enter__()
        return self

    def __exit__(self, *a):
        self.blk.__exit__(None, None, None)
        return False

    def eng(self, ek):
        st = self.S
        snap = self.snap

        def deco(f):
            def body(e):
                for e2, v in snap.items():
                    if v > 0:
                        e.wait_ge(st.sem(f"bar_{e2}"), v)
                last = f(e)
                if last is None:
                    last = e.nop()
                last.then_inc(st.sem(f"bar_{ek}"), 1)
            getattr(self.blk, ENG_NAME[ek])(body)
            st.cnt[f"bar_{ek}"] += 1
        return deco


class Chain:

    def __init__(self, st, prefix, nlanes=2, recycle=None):
        self.S = st
        self.prefix = prefix
        self.nlanes = nlanes
        self.steps = []
        pool = list(recycle) if recycle else []
        self.names = {}
        for L in range(nlanes):
            for e in BARS:
                nm = pool.pop(0) if pool else f"{prefix}{L}_{e}"
                self.names[(L, e)] = nm
                st.sem(nm)

    def add(self, ek, fn, lane=0, after=None):
        self.steps.append((lane, ek, fn, list(after or [])))
        return len(self.steps) - 1

    def emit(self, ph):
        st = self.S
        base = {(L, e): st.cnt[self.names[(L, e)]]
                for L in range(self.nlanes) for e in BARS}
        counts = dict(base)
        after_cnt = []
        for lane, ek, fn, deps in self.steps:
            counts[(lane, ek)] += 1
            after_cnt.append(dict(counts))
        prev_of = []
        last_in_lane = {}
        for i, (lane, ek, fn, deps) in enumerate(self.steps):
            prev_of.append(last_in_lane.get(lane))
            last_in_lane[lane] = i
        for ek in BARS:
            idxs = [i for i, (l, e, _, _) in enumerate(self.steps) if e == ek]
            if not idxs:
                continue

            def mk(my, my_idxs):
                def body(e):
                    for i in my_idxs:
                        L, _, fn, deps = self.steps[i]
                        j = prev_of[i]
                        if j is not None:
                            tgt = after_cnt[j]
                            for x in BARS:
                                if x != my and tgt[(L, x)] > base[(L, x)]:
                                    e.wait_ge(self.S.sem(self.names[(L, x)]),
                                              tgt[(L, x)])
                        for j2 in deps:
                            l2, e2 = self.steps[j2][0], self.steps[j2][1]
                            if not (l2 == L and j is not None and j2 <= j):
                                e.wait_ge(self.S.sem(self.names[(l2, e2)]),
                                          after_cnt[j2][(l2, e2)])
                        last = fn(e)
                        if last is None:
                            last = e.nop()
                        last.then_inc(self.S.sem(self.names[(L, my)]), 1)
                    return None
                return body
            ph.eng(ek)(mk(ek, idxs))
        for k, v in counts.items():
            st.cnt[self.names[(k[0], k[1])]] = v


def build_program(nstage=99):
    nc = bass.Bass("TRN2", target_bir_lowering=False)

    def dI(name, shape, dt=F32):
        return nc.dram_tensor(name, shape, dt, kind="ExternalInput")

    x_fm = dI("x_fm", [DM, TC])
    nonce = dI("nonce", [1, 4], I32)
    gw_r = dI("gw_r", [NIC, 128, DM])
    uw_r = dI("uw_r", [NIC, 128, DM])
    dw_r = dI("dw_r", [NIC, 128, DM])
    gs_r = dI("gs_r", [128, NIC])
    us_r = dI("us_r", [128, NIC])
    ds_r = dI("ds_r", [128, FC])
    wq_r = dI("wq_r", [HPC, 128, DM])
    wk_r = dI("wk_r", [HPC, 128, DM])
    wv_r = dI("wv_r", [HPC, 128, DM])
    wgt_r = dI("wgt_r", [HPC, 128, DM])
    wab_r = dI("wab_r", [DM, 16])
    abb_r = dI("abb_r", [1, 16])
    wo_r = dI("wo_r", [128, HPC * DM])
    qcw_r = dI("qcw_r", [128, 32])
    kcw_r = dI("kcw_r", [128, 32])
    vcw_r = dI("vcw_r", [128, 32])
    qcb_r = dI("qcb_r", [128, HPC])
    kcb_r = dI("kcb_r", [128, HPC])
    vcb_r = dI("vcb_r", [128, HPC])
    lnw_r = dI("lnw_r", [128, 1])
    lnb_r = dI("lnb_r", [128, 1])
    ident_b = dI("ident_b", [128, 128], BF16)
    ident_fr = dI("ident_fr", [128, 128])
    tri_inc = dI("tri_inc", [128, 128])
    tri_suf = dI("tri_suf", [128, 128])
    nmask_ts = dI("nmask_ts", [128, 128], BF16)
    nmask_st = dI("nmask_st", [128, 128], BF16)
    qmask_st = dI("qmask_st", [128, 128], BF16)

    out_fm = nc.dram_tensor("out_fm", [DM, TC], F32, kind="ExternalOutput")

    hdn_d = nc.dram_tensor("hdn_d", [INTER, TC], BF16)
    xh2_d = nc.dram_tensor("xh2_d", [DM, TC], BF16)
    p_d = [nc.dram_tensor(f"p{i}_d", [HPC * 128, T], BF16) for i in range(3)]
    qkv_d = [nc.dram_tensor(f"qkv{i}_d", [HPC * 128, T], BF16) for i in range(3)]
    g_d = nc.dram_tensor("g_d", [HPC * 128, T], BF16)
    oln_d = nc.dram_tensor("oln_d", [HPC * 128, T], BF16)
    wop_d = nc.dram_tensor("wop_d", [T, DM], BF16)
    rsm_d = nc.dram_tensor("rsm_d", [TC, DM], BF16)
    rsp_d = nc.dram_tensor("rsp_d", [TC, DM], BF16)
    ab_d = nc.dram_tensor("ab_d", [16, T], F32)
    am_d = nc.dram_tensor("am_d", [256, 128], F32)
    nl_d = nc.dram_tensor("nl_d", [256, 128], F32)
    lc_d = nc.dram_tensor("lc_d", [256, 128], F32)
    xsh = nc.dram_tensor("xsh", [2, DM, TC], BF16, addr_space="Shared")
    inbox = nc.dram_tensor("inbox", [2, TC, DM], BF16, addr_space="Shared")
    flags = nc.dram_tensor("flags", [2, 4], I32, addr_space="Shared")

    with ExitStack() as stack:
        st = St(nc, stack)
        E = stack.enter_context

        resid = E(nc.sbuf_tensor("resid", [128, FC * TC], BF16))
        gs_sb = E(nc.sbuf_tensor("gs_sb", [128, NIC], F32))
        us_sb = E(nc.sbuf_tensor("us_sb", [128, NIC], F32))
        ds5_sb = E(nc.sbuf_tensor("ds5_sb", [128, FC], F32))
        lnw_sb = E(nc.sbuf_tensor("lnw_sb", [128, 1], F32))
        lnb_sb = E(nc.sbuf_tensor("lnb_sb", [128, 1], F32))
        ones_r = E(nc.sbuf_tensor("ones_r", [1, 128], F32))
        ones_c = E(nc.sbuf_tensor("ones_c", [128, 1], BF16))
        nonce_sb = E(nc.sbuf_tensor("nonce_sb", [1, 4], I32))
        flag_sb = E(nc.sbuf_tensor("flag_sb", [2, 4], I32))
        gam_c = E(nc.sbuf_tensor("gam_c", [128, NCHK * HPC], F32))
        bgp_c = E(nc.sbuf_tensor("bgp_c", [128, NCHK * HPC], F32))
        bco_c = E(nc.sbuf_tensor("bco_c", [128, NCHK * HPC], F32))
        gsf_c = E(nc.sbuf_tensor("gsf_c", [128, NCHK * HPC], F32))
        gnd_c = E(nc.sbuf_tensor("gnd_c", [128, NCHK * HPC], F32))
        eps_c = E(nc.sbuf_tensor("eps_c", [128, 1], F32))

        with Phase(st, "setup") as ph:
            @ph.eng("g")
            def _(g):
                i = g.dma_start(
                    out=resid[:, :].rearrange("p (c t) -> p c t", c=FC),
                    in_=x_fm.ap().rearrange("(c p) t -> p c t", p=128))
                st.inc(i, "wg", 16)
                for src, dst in ((gs_r, gs_sb), (us_r, us_sb), (ds_r, ds5_sb),
                                 (lnw_r, lnw_sb), (lnb_r, lnb_sb)):
                    st.inc(g.dma_start(out=dst[:, :], in_=src.ap()), "wg", 16)
                st.inc(g.dma_start(out=nonce_sb[:, :], in_=nonce.ap()), "wg", 16)
                g.wait_ge(st.sem("wg"), st.cnt["wg"])
                return None

            @ph.eng("v")
            def _(v):
                v.memset(ones_r[:, :], 1.0)
                v.memset(ones_c[:, :], 1.0)
                v.memset(eps_c[:, :], 1e-5)
                v.wait_ge(st.sem("wg"), st.cnt["wg"])
                return v.tensor_scalar(ds5_sb[:, :], ds5_sb[:, :], 0.5, None, ALU.mult)

        def emit_ln(xh_sb, to_dram, tag):
            with ExitStack() as ps:
                e2 = ps.enter_context
                xsq = e2(nc.sbuf_tensor(f"{tag}_xsq", [128, 4 * 512], BF16))
                srow_s = e2(nc.sbuf_tensor(f"{tag}_rs", [1, 2048], F32))[:, :]
                srow_q = e2(nc.sbuf_tensor(f"{tag}_rq", [1, 2048], F32))[:, :]
                vrow = e2(nc.sbuf_tensor(f"{tag}_rv", [1, 2048], F32))[:, :]
                rrow = e2(nc.sbuf_tensor(f"{tag}_rr", [1, 2048], F32))[:, :]
                nbrow = e2(nc.sbuf_tensor(f"{tag}_rn", [1, 2048], F32))[:, :]
                rb = e2(nc.sbuf_tensor(f"{tag}_rb", [128, 2048], BF16))
                nb = e2(nc.sbuf_tensor(f"{tag}_nb", [128, 2048], BF16))
                t1b = e2(nc.sbuf_tensor(f"{tag}_t1", [128, 4 * 512], BF16))
                ps_s = e2(nc.psum_tensor(f"{tag}ps_s", [128, 512], F32))
                ps_q = e2(nc.psum_tensor(f"{tag}ps_q", [128, 512], F32))
                ps_b = e2(nc.psum_tensor(f"{tag}ps_b", [128, 512], F32))
                ps_n = e2(nc.psum_tensor(f"{tag}ps_n", [128, 512], F32))

                with Phase(st, tag + "a") as ph:
                    bq = st.base("lnsq")
                    bqc = st.base("lnqc")
                    bm = st.base("lnmm")
                    brw = st.base("lnrow")

                    @ph.eng("s")
                    def _(s):
                        n = 0
                        for tsl in range(4):
                            for fc in range(FC):
                                n += 1
                                if n > 4:
                                    s.wait_ge(st.sem("lnqc"), bqc + n - 4)
                                i = s.activation(
                                    xsq[:, ((n - 1) % 4) * 512:((n - 1) % 4) * 512 + 512],
                                    resid[:, fc * TC + tsl * 512:fc * TC + tsl * 512 + 512],
                                    AF.Square)
                                st.inc(i, "lnsq", 1)
                        return None

                    @ph.eng("t")
                    def _(t):
                        n = 0
                        for tsl in range(4):
                            if tsl > 0:
                                t.wait_ge(st.sem("lnrow"), brw + tsl)
                            for fc in range(FC):
                                t.matmul(ps_s[0:1, :], ones_c[:, 0:1],
                                         resid[:, fc * TC + tsl * 512:fc * TC + tsl * 512 + 512],
                                         start=(fc == 0), stop=(fc == FC - 1))
                            for fc in range(FC):
                                n += 1
                                t.wait_ge(st.sem("lnsq"), bq + n)
                                i = t.matmul(ps_q[0:1, :], ones_c[:, 0:1],
                                             xsq[:, ((n - 1) % 4) * 512:((n - 1) % 4) * 512 + 512],
                                             start=(fc == 0), stop=(fc == FC - 1))
                                st.inc(i, "lnqc", 1)
                            st.inc(t.nop(), "lnmm", 1)
                        return None

                    @ph.eng("v")
                    def _(v):
                        for tsl in range(4):
                            v.wait_ge(st.sem("lnmm"), bm + tsl + 1)
                            v.tensor_scalar(srow_s[:, tsl * 512:tsl * 512 + 512],
                                            ps_s[0:1, :], 1.0 / DM, None, ALU.mult)
                            i = v.tensor_scalar(srow_q[:, tsl * 512:tsl * 512 + 512],
                                                ps_q[0:1, :], 1.0 / DM, None, ALU.mult)
                            st.inc(i, "lnrow", 1)
                        return None

                with Phase(st, tag + "b") as ph:
                    br2 = st.base("lnr2")
                    bbc = st.base("lnbc")
                    bbv = st.base("lnbv")
                    bxh = st.base("lnxh")
                    bst = st.base("lnst")

                    @ph.eng("v")
                    def _(v):
                        mu = srow_s
                        e2r = srow_q
                        v.tensor_mul(vrow[0:1, :], mu, mu)
                        i = v.tensor_sub(vrow[0:1, :], e2r, vrow[0:1, :])
                        st.inc(i, "lnr2", 1)
                        v.wait_ge(st.sem("lnr2"), br2 + 2)
                        i = v.scalar_tensor_tensor(nbrow[0:1, :], mu, -1.0, rrow[0:1, :],
                                                   ALU.mult, ALU.mult)
                        st.inc(i, "lnr2", 1)
                        for tsl in range(4):
                            v.wait_ge(st.sem("lnbc"), bbc + tsl + 1)
                            v.tensor_copy(rb[:, tsl * 512:tsl * 512 + 512], ps_b[:, :])
                            i = v.tensor_copy(nb[:, tsl * 512:tsl * 512 + 512], ps_n[:, :])
                            st.inc(i, "lnbv", 1)
                        n = 0
                        for tsl in range(4):
                            for fc in range(FC):
                                n += 1
                                src = resid[:, fc * TC + tsl * 512:fc * TC + tsl * 512 + 512]
                                if xh_sb is not None:
                                    dst = xh_sb[:, fc * TC + tsl * 512:fc * TC + tsl * 512 + 512]
                                else:
                                    dst = t1b[:, ((n - 1) % 4) * 512:((n - 1) % 4) * 512 + 512]
                                    if n > 4:
                                        v.wait_ge(st.sem("lnst"), bst + (n - 4) * 16)
                                v.tensor_mul(dst, src, rb[:, tsl * 512:tsl * 512 + 512])
                                i = v.tensor_add(dst, dst, nb[:, tsl * 512:tsl * 512 + 512])
                                st.inc(i, "lnxh", 1)
                        return None

                    @ph.eng("s")
                    def _(s):
                        s.wait_ge(st.sem("lnr2"), br2 + 1)
                        s.activation(vrow[0:1, :], vrow[0:1, :], AF.Ln, bias=eps_c[0:1, 0:1])
                        i = s.activation(rrow[0:1, :], vrow[0:1, :], AF.Exp, scale=-0.5)
                        st.inc(i, "lnr2", 1)
                        return None

                    @ph.eng("t")
                    def _(t):
                        t.wait_ge(st.sem("lnr2"), br2 + 3)
                        for tsl in range(4):
                            if tsl > 0:
                                t.wait_ge(st.sem("lnbv"), bbv + tsl)
                            t.matmul(ps_b[:, :], ones_r[0:1, 0:128],
                                     rrow[0:1, tsl * 512:tsl * 512 + 512], start=True, stop=True)
                            last = t.matmul(ps_n[:, :], ones_r[0:1, 0:128],
                                            nbrow[0:1, tsl * 512:tsl * 512 + 512], start=True, stop=True)
                            st.inc(last, "lnbc", 1)
                        return None

                    if to_dram is not None:
                        @ph.eng("y")
                        def _(y):
                            n = 0
                            for tsl in range(4):
                                for fc in range(FC):
                                    n += 1
                                    y.wait_ge(st.sem("lnxh"), bxh + n)
                                    i = y.dma_start(
                                        out=to_dram[fc * 128:(fc + 1) * 128, tsl * 512:tsl * 512 + 512],
                                        in_=t1b[:, ((n - 1) % 4) * 512:((n - 1) % 4) * 512 + 512])
                                    st.inc(i, "lnst", 16)
                            y.wait_ge(st.sem("lnst"), st.cnt["lnst"])
                            return None

        def emit_ffn(final, tag, ngroups=NG):
            with ExitStack() as fs:
                ef = fs.enter_context
                xh = ef(nc.sbuf_tensor(f"{tag}_xh", [128, FC * TC], BF16))
                emit_ln(xh_sb=xh, to_dram=None, tag=tag + "l")
                for grp in range(ngroups):
                    ic0 = grp * GIC
                    with ExitStack() as gsk:
                        eg = gsk.enter_context
                        dt_sb = eg(nc.sbuf_tensor(f"{tag}dnw{grp}", [128, GIC * DM], BF16))
                        gsk2 = ExitStack()
                        wbuf = gsk2.enter_context(nc.sbuf_tensor(f"{tag}guw{grp}", [128, 4 * DM], BF16))
                        sgb = gsk2.enter_context(nc.sbuf_tensor(f"{tag}gusg{grp}", [128, 2 * 512], BF16))
                        hrow = gsk2.enter_context(nc.sbuf_tensor(f"{tag}guh{grp}", [128, 4 * 512], BF16))
                        psg = gsk2.enter_context(nc.psum_tensor(f"{tag}pg{grp}", [128, 1024], F32))
                        psu = gsk2.enter_context(nc.psum_tensor(f"{tag}pu{grp}", [128, 1024], F32))

                        with Phase(st, f"{tag}gu{grp}") as ph:
                            bw = st.base("wg")
                            bm = st.base("gumm")
                            bsg = st.base("gusg")
                            bev = st.base("guev")
                            bst = st.base("gust")

                            @ph.eng("g")
                            def _(g):
                                for j in range(GIC):
                                    ic = ic0 + j
                                    sl = (j % 2) * 2 * DM
                                    if j >= 2:
                                        g.wait_ge(st.sem("gumm"), bm + (j - 1) * 8)
                                    st.inc(g.dma_start(out=wbuf[:, sl:sl + DM], in_=gw_r.ap()[ic]), "wg", 16)
                                    st.inc(g.dma_start(out=wbuf[:, sl + DM:sl + 2 * DM], in_=uw_r.ap()[ic]), "wg", 16)
                                for j in range(GIC):
                                    st.inc(g.dma_start(out=dt_sb[:, j * DM:(j + 1) * DM],
                                                       in_=dw_r.ap()[ic0 + j]), "wg", 16)
                                return None

                            @ph.eng("t")
                            def _(t):
                                for j in range(GIC):
                                    sl = (j % 2) * 2 * DM
                                    t.wait_ge(st.sem("wg"), bw + (j + 1) * 32)
                                    for tsl in range(4):
                                        n = j * 4 + tsl
                                        ps_sl = (tsl % 2) * 512
                                        if n >= 2:
                                            t.wait_ge(st.sem("guev"), bev + n - 1)
                                        for fc in range(FC):
                                            i = t.matmul(psg[:, ps_sl:ps_sl + 512],
                                                         wbuf[:, sl + fc * 128:sl + fc * 128 + 128],
                                                         xh[:, fc * TC + tsl * 512:fc * TC + tsl * 512 + 512],
                                                         start=(fc == 0), stop=(fc == FC - 1))
                                        st.inc(i, "gumm", 1)
                                        for fc in range(FC):
                                            i = t.matmul(psu[:, ps_sl:ps_sl + 512],
                                                         wbuf[:, sl + DM + fc * 128:sl + DM + fc * 128 + 128],
                                                         xh[:, fc * TC + tsl * 512:fc * TC + tsl * 512 + 512],
                                                         start=(fc == 0), stop=(fc == FC - 1))
                                        st.inc(i, "gumm", 1)
                                return None

                            @ph.eng("s")
                            def _(s):
                                for j in range(GIC):
                                    ic = ic0 + j
                                    for tsl in range(4):
                                        n = j * 4 + tsl
                                        ps_sl = (tsl % 2) * 512
                                        s.wait_ge(st.sem("gumm"), bm + n * 2 + 1)
                                        if n >= 2:
                                            s.wait_ge(st.sem("guev"), bev + n - 1)
                                        i = s.activation(sgb[:, ps_sl:ps_sl + 512],
                                                         psg[:, ps_sl:ps_sl + 512],
                                                         AF.Silu, scale=gs_sb[:, ic:ic + 1])
                                        st.inc(i, "gusg", 1)
                                return None

                            @ph.eng("v")
                            def _(v):
                                for j in range(GIC):
                                    ic = ic0 + j
                                    for tsl in range(4):
                                        n = j * 4 + tsl
                                        ps_sl = (tsl % 2) * 512
                                        v.wait_ge(st.sem("gumm"), bm + n * 2 + 2)
                                        v.wait_ge(st.sem("gusg"), bsg + n + 1)
                                        if n >= 4:
                                            v.wait_ge(st.sem("gust"), bst + (n - 3) * 16)
                                        i = v.scalar_tensor_tensor(
                                            hrow[:, (n % 4) * 512:(n % 4) * 512 + 512],
                                            psu[:, ps_sl:ps_sl + 512],
                                            us_sb[:, ic:ic + 1],
                                            sgb[:, ps_sl:ps_sl + 512],
                                            ALU.mult, ALU.mult)
                                        st.inc(i, "guev", 1)
                                return None

                            @ph.eng("y")
                            def _(y):
                                for j in range(GIC):
                                    ic = ic0 + j
                                    for tsl in range(4):
                                        n = j * 4 + tsl
                                        y.wait_ge(st.sem("guev"), bev + n + 1)
                                        i = y.dma_start(
                                            out=hdn_d[ic * 128:(ic + 1) * 128, tsl * 512:tsl * 512 + 512],
                                            in_=hrow[:, (n % 4) * 512:(n % 4) * 512 + 512])
                                        st.inc(i, "gust", 16)
                                y.wait_ge(st.sem("gust"), st.cnt["gust"])
                                return None

                        gsk2.close()
                        dsk = ExitStack()
                        hwin = dsk.enter_context(nc.sbuf_tensor(f"{tag}dnh{grp}", [128, 2 * GIC * 512], BF16))
                        of32 = dsk.enter_context(nc.sbuf_tensor(f"{tag}dno{grp}", [128, 1024], F32)) if (final and grp == NG - 1) else None
                        psd = dsk.enter_context(nc.psum_tensor(f"{tag}dnp{grp}", [128, 1024], F32))
                        with Phase(st, f"{tag}dn{grp}") as ph:
                            bh = st.base("dnhl")
                            bm = st.base("dnmm")
                            bev = st.base("dnev")
                            bos = st.base("dnos")

                            @ph.eng("y")
                            def _(y):
                                def stores_for(tsl2):
                                    for hc in range(FC):
                                        n2 = tsl2 * FC + hc + 1
                                        y.wait_ge(st.sem("dnev"), bev + n2)
                                        i2 = y.dma_start(
                                            out=out_fm[hc * 128:(hc + 1) * 128, tsl2 * 512:tsl2 * 512 + 512],
                                            in_=of32[:, (n2 % 2) * 512:(n2 % 2) * 512 + 512])
                                        st.inc(i2, "dnos", 16)
                                for tsl in range(4):
                                    if of32 is not None and tsl >= 1:
                                        stores_for(tsl - 1)
                                    if tsl >= 2:
                                        y.wait_ge(st.sem("dnmm"), bm + (tsl - 1) * FC)
                                    hb0 = (tsl % 2) * GIC * 512
                                    i = y.dma_start(
                                        out=hwin[:, hb0:hb0 + GIC * 512].rearrange("p (a b) -> p a b", a=GIC),
                                        in_=hdn_d.ap().rearrange("(a p) t -> p a t", p=128)
                                            [:, ic0:ic0 + GIC, tsl * 512:tsl * 512 + 512])
                                    st.inc(i, "dnhl", 16)
                                if of32 is not None:
                                    stores_for(3)
                                    y.wait_ge(st.sem("dnos"), st.cnt["dnos"])
                                return None

                            @ph.eng("t")
                            def _(t):
                                t.wait_ge(st.sem("wg"), st.cnt["wg"])
                                n = 0
                                for tsl in range(4):
                                    t.wait_ge(st.sem("dnhl"), bh + (tsl + 1) * 16)
                                    for hc in range(FC):
                                        n += 1
                                        if n > 2:
                                            t.wait_ge(st.sem("dnev"), bev + n - 2)
                                        for j in range(GIC):
                                            i = t.matmul(
                                                psd[:, (n % 2) * 512:(n % 2) * 512 + 512],
                                                dt_sb[:, j * DM + hc * 128:j * DM + hc * 128 + 128],
                                                hwin[:, (tsl % 2) * GIC * 512 + j * 512:(tsl % 2) * GIC * 512 + j * 512 + 512],
                                                start=(j == 0), stop=(j == GIC - 1))
                                        st.inc(i, "dnmm", 1)
                                return None

                            @ph.eng("v")
                            def _(v):
                                n = 0
                                for tsl in range(4):
                                    for hc in range(FC):
                                        n += 1
                                        v.wait_ge(st.sem("dnmm"), bm + n)
                                        rs = resid[:, hc * TC + tsl * 512:hc * TC + tsl * 512 + 512]
                                        if of32 is not None:
                                            if n > 2:
                                                v.wait_ge(st.sem("dnos"), bos + (n - 2) * 16)
                                            i = v.scalar_tensor_tensor(
                                                of32[:, (n % 2) * 512:(n % 2) * 512 + 512],
                                                psd[:, (n % 2) * 512:(n % 2) * 512 + 512],
                                                ds5_sb[:, hc:hc + 1], rs, ALU.mult, ALU.add)
                                        else:
                                            i = v.scalar_tensor_tensor(
                                                rs, psd[:, (n % 2) * 512:(n % 2) * 512 + 512],
                                                ds5_sb[:, hc:hc + 1], rs, ALU.mult, ALU.add)
                                        st.inc(i, "dnev", 1)
                                return None
                        dsk.close()

        def dump_resid():
            with Phase(st, "dump") as ph:
                @ph.eng("g")
                def _(g):
                    i = g.dma_start(
                        out=out_fm.ap().rearrange("(c p) t -> p c t", p=128),
                        in_=resid[:, :].rearrange("p (c t) -> p c t", c=FC))
                    st.inc(i, "od", 16)
                    g.wait_ge(st.sem("od"), st.cnt["od"])
                    return None

        emit_ffn(final=False, tag="f1")
        if nstage == 1:
            dump_resid()
            return nc

        emit_ln(xh_sb=None, to_dram=xh2_d, tag="ln2")
        with Phase(st, "ex1") as ph:
            @ph.eng("g")
            def _(g):
                bex = st.base("ex1s")
                for arm in nc.switch_core_id(engines=[g], n=128):
                    pos = _shard_of_arm(int(arm)) & 1
                    g.dma_start(out=xsh.ap()[pos], in_=xh2_d.ap()).then_inc(st.sem("ex1s"), 16)
                    g.wait_ge(st.sem("ex1s"), bex + 16)
                    g.dma_start(out=flags[pos:pos + 1, 0:1], in_=nonce_sb[0:1, 0:1]).then_inc(st.sem("ex1s"), 16)
                st.cnt["ex1s"] += 32
                g.wait_ge(st.sem("ex1s"), st.cnt["ex1s"])
                with g.register("dn") as dn, g.register("exv") as exv, \
                        g.register("bk1") as bk:
                    g.reg_mov(dn, 0)
                    g.reg_mov(exv, st.cnt["ex1s"])
                    with g.While(lambda: g.snap(dn) == 0):
                        g.dma_start(out=flag_sb[:, :], in_=flags[:, :]).then_inc(st.sem("ex1s"), 16)
                        g.reg_add(exv, exv, 16)
                        g.wait_ge(st.sem("ex1s"), exv)
                        v0 = g.value_load(flag_sb[0:1, 0:1])
                        v1 = g.value_load(flag_sb[1:2, 0:1])
                        vn = g.value_load(nonce_sb[0:1, 0:1])
                        c0 = (g.snap(v0) == g.snap(vn))
                        c1 = (g.snap(v1) == g.snap(vn))
                        g.reg_mov(dn, 1)
                        with g.If(g.snap(c0) == 0):
                            g.reg_mov(dn, 0)
                        with g.If(g.snap(c1) == 0):
                            g.reg_mov(dn, 0)
                        with g.If(g.snap(dn) == 0):
                            for _sp in range(600):
                                g.reg_add(bk, bk, 1)
                return None
        if nstage == 2:
            dump_resid()
            return nc

        with ExitStack() as ls:
            el = ls.enter_context
            xhh = el(nc.sbuf_tensor("pj_xh", [128, FC * TC], BF16))
            wab_sb = el(nc.sbuf_tensor("wab_sb", [128, FC * 16], BF16))
            abb_sb = el(nc.sbuf_tensor("abb_sb", [1, 16], BF16))
            ones_rb = el(nc.sbuf_tensor("ones_rb", [1, 512], BF16))
            wtl = el(nc.sbuf_tensor("pj_w", [128, 2 * DM], BF16))
            prow = el(nc.sbuf_tensor("pj_p", [128, 2 * TC], BF16))
            psp = el(nc.psum_tensor("pj_ps", [128, 2048], F32))
            psp2 = el(nc.psum_tensor("pj_ps2", [128, 2048], F32))
            psab = psp[0:16, 0:512]
            abst = el(nc.sbuf_tensor("pj_abst", [16, 2 * 512], F32))
            pad = el(nc.sbuf_tensor("cv_pad", [128, 3 + T], BF16))
            cw_sb = el(nc.sbuf_tensor("cw_sb", [128, 96], F32))
            cb_sb = el(nc.sbuf_tensor("cb_sb", [128, 24], F32))
            st1 = el(nc.sbuf_tensor("cv_s1", [128, T], BF16))
            st2 = el(nc.sbuf_tensor("cv_s2", [128, T], BF16))
            outb = el(nc.sbuf_tensor("cv_o", [128, 2 * T], BF16))
            WSRC = [wq_r, wk_r, wv_r, wgt_r]
            DDST = [p_d[0], p_d[1], p_d[2], g_d]

            def pj_ev_wait(e, n_needed, bev, bevg):
                if n_needed <= 0:
                    return
                if n_needed <= 24:
                    e.wait_ge(st.sem("pjev"), bev + n_needed)
                else:
                    e.wait_ge(st.sem("pjev"), bev + 24)
                    e.wait_ge(st.sem("pjevg"), bevg + n_needed - 24)

            for th in range(2):
                with Phase(st, f"pj{th}") as ph:
                    bxl = st.base("pjxl")
                    bw = st.base("wg")
                    bm = st.base("pjmm")
                    bev = st.base("pjev")
                    bevg = st.base("pjevg")
                    bst = st.base("pjst")
                    bab = st.base("pjab")
                    bao = st.base("pjao")
                    bfl = st.base("pjfl")
                    blc = st.base("cvl")
                    bcc = st.base("cvc")
                    bcs = st.base("cvs")
                    bcv = st.base("cvo")
                    bck = st.base("cvk")
                    bcw = st.base("cvw")

                    def conv_store(y, mm):
                        y.wait_ge(st.sem("cvs"), bcs + mm + 1)
                        if mm // 8 == 1:
                            y.wait_ge(st.sem("cvk"), bck + (mm - 8) + 1)
                        sj, oj = divmod(mm, 8)
                        i = y.dma_start(out=qkv_d[sj][oj * 128:(oj + 1) * 128, :],
                                        in_=outb[:, (mm % 2) * T:(mm % 2) * T + T])
                        st.inc(i, "cvo", 16)

                    def conv_y(y, m):
                        si, oc = divmod(m, 8)
                        if m > 0:
                            y.wait_ge(st.sem("cvc"), bcc + m)
                        y.wait_ge(st.sem("pjst"), bst + (m + 1) * 16)
                        i = y.dma_start(out=pad[:, 3:3 + T],
                                        in_=p_d[si][oc * 128:(oc + 1) * 128, :])
                        st.inc(i, "cvl", 16)
                        if m >= 2:
                            conv_store(y, m - 2)

                    def conv_v(v, m):
                        si, oc = divmod(m, 8)
                        if m == 0:
                            v.memset(pad[:, 0:3], 0.0)
                            v.wait_ge(st.sem("cvw"), bcw + 6 * 16)
                        w0 = cw_sb[:, si * 32 + oc * 4 + 0:si * 32 + oc * 4 + 1]
                        w1 = cw_sb[:, si * 32 + oc * 4 + 1:si * 32 + oc * 4 + 2]
                        w2 = cw_sb[:, si * 32 + oc * 4 + 2:si * 32 + oc * 4 + 3]
                        w3 = cw_sb[:, si * 32 + oc * 4 + 3:si * 32 + oc * 4 + 4]
                        v.wait_ge(st.sem("cvl"), blc + (m + 1) * 16)
                        if m >= 1:
                            v.wait_ge(st.sem("cvs"), bcs + m)
                        v.tensor_scalar(st1[:, :], pad[:, 0:T], w0, None, ALU.mult)
                        v.scalar_tensor_tensor(st2[:, :], pad[:, 1:1 + T], w1, st1[:, :], ALU.mult, ALU.add)
                        v.scalar_tensor_tensor(st1[:, :], pad[:, 2:2 + T], w2, st2[:, :], ALU.mult, ALU.add)
                        i = v.scalar_tensor_tensor(st2[:, :], pad[:, 3:3 + T], w3, st1[:, :], ALU.mult, ALU.add)
                        st.inc(i, "cvc", 1)
                        if si == 1:
                            v.wait_ge(st.sem("cvs"), bcs + m + 1)
                            i = v.tensor_scalar(outb[:, (m % 2) * T:(m % 2) * T + T],
                                                outb[:, (m % 2) * T:(m % 2) * T + T],
                                                RD, None, ALU.mult)
                            st.inc(i, "cvk", 1)

                    def conv_s(s, m):
                        si, oc = divmod(m, 8)
                        s.wait_ge(st.sem("cvc"), bcc + m + 1)
                        if m >= 2:
                            s.wait_ge(st.sem("cvo"), bcv + (m - 1) * 16)
                            if (m - 2) // 8 == 1:
                                s.wait_ge(st.sem("cvk"), bck + (m - 2 - 8) + 1)
                        i = s.activation(outb[:, (m % 2) * T:(m % 2) * T + T], st2[:, :],
                                         AF.Silu, bias=cb_sb[:, si * HPC + oc:si * HPC + oc + 1])
                        st.inc(i, "cvs", 1)

                    @ph.eng("y")
                    def _(y):
                        i = y.dma_start(
                            out=xhh[:, :].rearrange("p (c t) -> p c t", c=FC),
                            in_=xsh.ap()[th].rearrange("(c p) t -> p c t", p=128))
                        st.inc(i, "pjxl", 16)
                        i = y.dma_start(out=prow[0:16, 128:256],
                                        in_=xhh[0:16, FC * TC - 128:FC * TC])
                        st.inc(i, "pjxl", 16)
                        for n in range(32):
                            pj_ev_wait(y, n + 1, bev, bevg)
                            si, oc = divmod(n, 8)
                            i = y.dma_start(
                                out=DDST[si][oc * 128:(oc + 1) * 128, th * TC:th * TC + TC],
                                in_=prow[:, (n % 2) * TC:(n % 2) * TC + TC])
                            st.inc(i, "pjst", 16)
                            if th == 1 and n >= 8:
                                conv_y(y, n - 8)
                        for tsl in range(4):
                            y.wait_ge(st.sem("pjab"), bab + tsl + 1)
                            i = y.dma_start(
                                out=ab_d[0:16, th * TC + tsl * 512:th * TC + tsl * 512 + 512],
                                in_=abst[0:16, (tsl % 2) * 512:(tsl % 2) * 512 + 512])
                            st.inc(i, "pjao", 16)
                        if th == 1:
                            for mm in (22, 23):
                                conv_store(y, mm)
                            y.wait_ge(st.sem("cvo"), st.cnt["cvo"])
                        y.wait_ge(st.sem("pjst"), st.cnt["pjst"])
                        y.wait_ge(st.sem("pjao"), st.cnt["pjao"])
                        return None

                    @ph.eng("g")
                    def _(g):
                        if th == 0:
                            st.inc(g.dma_start(
                                out=wab_sb[:, :].rearrange("p (c k) -> p c k", k=16),
                                in_=wab_r.ap().rearrange("(c p) k -> p c k", p=128)), "wg", 16)
                            st.inc(g.dma_start(out=abb_sb[:, :], in_=abb_r.ap()), "wg", 16)
                        for n in range(32):
                            si, oc = divmod(n, 8)
                            if n >= 2:
                                g.wait_ge(st.sem("pjmm"), bm + (n - 1) * 4)
                            i = g.dma_start(out=wtl[:, (n % 2) * DM:(n % 2) * DM + DM],
                                            in_=WSRC[si].ap()[oc])
                            st.inc(i, "wg", 16)
                            if n == 0:
                                i = g.dma_start(out=prow[0:16, 0:128],
                                                in_=wtl[0:16, DM - 128:DM])
                                st.inc(i, "pjfl", 16)
                            if th == 1 and n == 4:
                                for j, srcw in enumerate((qcw_r, kcw_r, vcw_r)):
                                    st.inc(g.dma_start(out=cw_sb[:, j * 32:j * 32 + 32], in_=srcw.ap()), "cvw", 16)
                                for j, srcw in enumerate((qcb_r, kcb_r, vcb_r)):
                                    st.inc(g.dma_start(out=cb_sb[:, j * HPC:j * HPC + HPC], in_=srcw.ap()), "cvw", 16)
                        return None

                    @ph.eng("t")
                    def _(t):
                        t.wait_ge(st.sem("pjxl"), bxl + 32)
                        woff = 32 if th == 0 else 0
                        for n in range(32):
                            pp = psp if n % 2 == 0 else psp2
                            t.wait_ge(st.sem("wg"), bw + woff + (n + 1) * 16)
                            if n == 0:
                                t.wait_ge(st.sem("pjfl"), bfl + 16)
                            pj_ev_wait(t, n - 1, bev, bevg)
                            for tsl in range(4):
                                for fc in range(FC):
                                    i = t.matmul(pp[:, tsl * 512:tsl * 512 + 512],
                                                 wtl[:, (n % 2) * DM + fc * 128:(n % 2) * DM + fc * 128 + 128],
                                                 xhh[:, fc * TC + tsl * 512:fc * TC + tsl * 512 + 512],
                                                 start=(fc == 0), stop=(fc == FC - 1))
                                st.inc(i, "pjmm", 1)
                        pj_ev_wait(t, 31, bev, bevg)
                        t.wait_ge(st.sem("pjrb"), st.base("pjrb"))
                        for tsl in range(4):
                            if tsl >= 1:
                                t.wait_ge(st.sem("pjab"), bab + tsl)
                            for fc in range(FC):
                                t.matmul(psab,
                                         wab_sb[:, fc * 16:fc * 16 + 16],
                                         xhh[:, fc * TC + tsl * 512:fc * TC + tsl * 512 + 512],
                                         start=(fc == 0), stop=False)
                            i = t.matmul(psab, abb_sb[0:1, 0:16],
                                         ones_rb[0:1, 0:512], start=False, stop=True)
                            st.inc(i, "pjmm", 1)
                        return None

                    @ph.eng("v")
                    def _(v):
                        if th == 0:
                            st.inc(v.memset(ones_rb[:, :], 1.0), "pjrb", 1)
                        for n in range(24):
                            pp = psp if n % 2 == 0 else psp2
                            v.wait_ge(st.sem("pjmm"), bm + (n * 4) + 4)
                            if n >= 2:
                                v.wait_ge(st.sem("pjst"), bst + (n - 1) * 16)
                            i = v.tensor_copy(prow[:, (n % 2) * TC:(n % 2) * TC + TC], pp[:, :])
                            st.inc(i, "pjev", 1)
                            if th == 1 and n >= 9:
                                conv_v(v, n - 9)
                        if th == 1:
                            for m in range(15, 24):
                                conv_v(v, m)
                        for tsl in range(4):
                            v.wait_ge(st.sem("pjmm"), bm + 128 + tsl + 1)
                            if tsl >= 2:
                                v.wait_ge(st.sem("pjao"), bao + (tsl - 1) * 16)
                            i = v.tensor_copy(abst[0:16, (tsl % 2) * 512:(tsl % 2) * 512 + 512],
                                              psab)
                            st.inc(i, "pjab", 1)
                        return None

                    @ph.eng("s")
                    def _(s):
                        if th == 1:
                            for m in range(14):
                                conv_s(s, m)
                        for n in range(24, 32):
                            pp = psp if n % 2 == 0 else psp2
                            s.wait_ge(st.sem("pjmm"), bm + (n * 4) + 4)
                            if n >= 26:
                                s.wait_ge(st.sem("pjst"), bst + (n - 1) * 16)
                            i = s.activation(prow[:, (n % 2) * TC:(n % 2) * TC + TC], pp[:, :],
                                             AF.Sigmoid)
                            st.inc(i, "pjevg", 1)
                            if th == 1:
                                conv_s(s, 14 + (n - 24))
                        if th == 1:
                            for m in (22, 23):
                                conv_s(s, m)
                        return None
        if nstage == 3:
            dump_resid()
            return nc

        if nstage == 4:
            dump_resid()
            return nc

        with ExitStack() as ls:
            el = ls.enter_context
            ab_rows = el(nc.sbuf_tensor("ab_rows", [16, T], F32))
            id_f = el(nc.sbuf_tensor("id_f", [128, 128], F32))
            tri_i = el(nc.sbuf_tensor("tri_i", [128, 128], F32))
            tri_s = el(nc.sbuf_tensor("tri_s", [128, 128], F32))
            amT = el(nc.sbuf_tensor("amT", [128, 256], F32))
            nlT = el(nc.sbuf_tensor("nlT", [128, 256], F32))
            lcT = el(nc.sbuf_tensor("lcT", [128, 256], F32))
            lt = el(nc.sbuf_tensor("ab_lt", [128, 512], F32))
            lc = el(nc.sbuf_tensor("ab_lc", [128, 512], F32))
            lsf = el(nc.sbuf_tensor("ab_ls", [128, 512], F32))
            lp = el(nc.sbuf_tensor("ab_lp", [128, 512], F32))
            am2 = el(nc.sbuf_tensor("ab_am", [128, 256], F32))
            nl2 = el(nc.sbuf_tensor("ab_nl", [128, 256], F32))
            lc2 = el(nc.sbuf_tensor("ab_lc2", [128, 256], F32))
            pst = el(nc.psum_tensor("ab_pt", [128, 512], F32))
            psc = el(nc.psum_tensor("ab_pc", [128, 512], F32))
            pss = el(nc.psum_tensor("ab_ps", [128, 512], F32))
            with Phase(st, "abp") as ph:
                bsg = st.base("absg")
                btr = st.base("abtr")
                bt2 = st.base("abt2")
                bmm = st.base("abmm")
                bcm = st.base("abcm")
                bcl = st.base("abcl")
                bwr = st.base("abwr")
                bld = st.base("abld")

                @ph.eng("s")
                def _(s):
                    s.wait_ge(st.sem("abld"), bld + 64)
                    s.activation(ab_rows[0:16, :], ab_rows[0:16, :], AF.Sigmoid)
                    i = s.activation(ab_rows[0:16, :], ab_rows[0:16, :], AF.Ln)
                    st.inc(i, "absg", 1)
                    s.wait_ge(st.sem("abcm"), bcm + 2)
                    lav = lc[:, :].rearrange("p (c j) -> p c j", j=16)[:, :, 0:8]
                    s.activation(gam_c[:, :].rearrange("p (c j) -> p c j", j=8), lav, AF.Exp)
                    s.activation(bgp_c[:, :], am2[:, :], AF.Exp)
                    lbv = lt[:, :].rearrange("p (c j) -> p c j", j=16)[:, :, 8:16]
                    s.activation(bco_c[:, :].rearrange("p (c j) -> p c j", j=8), lbv, AF.Exp)
                    lsv = lsf[:, :].rearrange("p (c j) -> p c j", j=16)[:, :, 0:8]
                    i = s.activation(gsf_c[:, :].rearrange("p (c j) -> p c j", j=8), lsv, AF.Exp)
                    st.inc(i, "absg", 1)
                    return None

                @ph.eng("t")
                def _(t):
                    t.wait_ge(st.sem("absg"), bsg + 1)
                    for q8 in range(4):
                        if q8 >= 1:
                            t.wait_ge(st.sem("abcl"), bcl + q8)
                        last = None
                        for cc in range(q8 * 8, q8 * 8 + 8):
                            last = t.transpose(pst[:, (cc % 8) * 16:(cc % 8) * 16 + 16],
                                               ab_rows[0:16, cc * 128:cc * 128 + 128],
                                               id_f[0:16, 0:16])
                        st.inc(last, "abtr", 1)
                    t.wait_ge(st.sem("abcl"), bcl + 4)
                    t.matmul(psc[:, :], tri_i[:, :], lt[:, :], start=True, stop=True)
                    i = t.matmul(pss[:, :], tri_s[:, :], lt[:, :], start=True, stop=True)
                    st.inc(i, "abmm", 1)
                    t.wait_ge(st.sem("abcm"), bcm + 2)
                    for k2 in range(2):
                        t.transpose(pst[:, 0:128], am2[:, k2 * 128:k2 * 128 + 128], id_f[:, :])
                        t.transpose(pst[:, 128:256], nl2[:, k2 * 128:k2 * 128 + 128], id_f[:, :])
                        i = t.transpose(pst[:, 256:384], lc2[:, k2 * 128:k2 * 128 + 128], id_f[:, :])
                        st.inc(i, "abt2", 1)
                        t.wait_ge(st.sem("abcl"), bcl + 5 + k2)
                    return None

                @ph.eng("v")
                def _(v):
                    for q8 in range(4):
                        v.wait_ge(st.sem("abtr"), btr + q8 + 1)
                        i = v.tensor_copy(lt[:, q8 * 128:q8 * 128 + 128], pst[:, 0:128])
                        st.inc(i, "abcl", 1)
                    v.wait_ge(st.sem("abmm"), bmm + 1)
                    v.tensor_copy(lc[:, :], psc[:, :])
                    v.tensor_copy(lsf[:, :], pss[:, :])
                    v.tensor_sub(lp[:, :], lc[:, :], lt[:, :])
                    lav_p = lp[:, :].rearrange("p (c j) -> p c j", j=16)[:, :, 0:8]
                    lbv = lt[:, :].rearrange("p (c j) -> p c j", j=16)[:, :, 8:16]
                    i = v.tensor_tensor(am2[:, :].rearrange("p (c j) -> p c j", j=8), lav_p, lbv, ALU.add)
                    st.inc(i, "abcm", 1)
                    lav_c = lc[:, :].rearrange("p (c j) -> p c j", j=16)[:, :, 0:8]
                    v.tensor_scalar(nl2[:, :].rearrange("p (c j) -> p c j", j=8), lav_c, -1.0, None, ALU.mult)
                    i = v.tensor_copy(lc2[:, :].rearrange("p (c j) -> p c j", j=8), lav_c)
                    st.inc(i, "abcm", 1)
                    v.wait_ge(st.sem("absg"), bsg + 2)
                    i = v.tensor_mul(gnd_c[:, :], gsf_c[:, :], gam_c[:, :])
                    st.inc(i, "abcm", 1)
                    for k2 in range(2):
                        v.wait_ge(st.sem("abt2"), bt2 + k2 + 1)
                        v.tensor_copy(amT[:, k2 * 128:k2 * 128 + 128], pst[:, 0:128])
                        v.tensor_copy(nlT[:, k2 * 128:k2 * 128 + 128], pst[:, 128:256])
                        i = v.tensor_copy(lcT[:, k2 * 128:k2 * 128 + 128], pst[:, 256:384])
                        st.inc(i, "abcl", 1)
                    return None

                @ph.eng("y")
                def _(y):
                    i = y.dma_start(out=ab_rows[0:16, :], in_=ab_d.ap())
                    st.inc(i, "abld", 16)
                    for srcc, dstc in ((ident_fr, id_f), (tri_inc, tri_i), (tri_suf, tri_s)):
                        st.inc(y.dma_start(out=dstc[:, :], in_=srcc.ap()), "abld", 16)
                    for k2 in range(2):
                        y.wait_ge(st.sem("abcl"), bcl + 5 + k2)
                        for srcT, dstD in ((amT, am_d), (nlT, nl_d), (lcT, lc_d)):
                            i = y.dma_start(out=dstD[k2 * 128:k2 * 128 + 128, :],
                                            in_=srcT[:, k2 * 128:k2 * 128 + 128])
                            st.inc(i, "abwr", 16)
                    y.wait_ge(st.sem("abwr"), st.cnt["abwr"])
                    return None
        if nstage == 5:
            dump_resid()
            return nc

        with ExitStack() as ls:
            el = ls.enter_context

            class LaneSet:
                pass
            sets = []
            for S2 in range(2):
                sn = LaneSet()
                sn.qt = el(nc.sbuf_tensor(f"sc_q{S2}", [128, 2 * 512], BF16))
                sn.kt = el(nc.sbuf_tensor(f"sc_k{S2}", [128, 2 * 512], BF16))
                sn.vt = el(nc.sbuf_tensor(f"sc_v{S2}", [128, 2 * 512], BF16))
                sn.Nb = el(nc.sbuf_tensor(f"sc_N{S2}", [128, 512], BF16))
                sn.Ntb = el(nc.sbuf_tensor(f"sc_Nt{S2}", [128, 512], BF16))
                sn.Sqb = el(nc.sbuf_tensor(f"sc_Sq{S2}", [128, 512], BF16))
                sn.Wb = [el(nc.sbuf_tensor(f"sc_W{S2}_{k}", [128, 512], BF16)) for k in range(7)]
                sn.Tb = [el(nc.sbuf_tensor(f"sc_T{S2}_{k}", [128, 512], BF16)) for k in range(2)]
                sn.scr1 = el(nc.sbuf_tensor(f"sc_x1{S2}", [128, 512], F32))
                sn.scr2 = el(nc.sbuf_tensor(f"sc_x2{S2}", [128, 512], F32))
                sn.ub = el(nc.sbuf_tensor(f"sc_u{S2}", [128, 512], BF16))
                sn.Rb = el(nc.sbuf_tensor(f"sc_R{S2}", [128, 512], BF16))
                sn.o1 = el(nc.sbuf_tensor(f"sc_o1{S2}", [128, 512], F32))
                sn.Ob = el(nc.sbuf_tensor(f"sc_O{S2}", [128, 512], F32))
                sn.z0 = el(nc.sbuf_tensor(f"sc_z0{S2}", [128, 512], F32))
                sn.zb = el(nc.sbuf_tensor(f"sc_zb{S2}", [128, 512], BF16))
                sn.Ks4 = el(nc.sbuf_tensor(f"sc_Ks{S2}", [128, 512], BF16))
                sn.olnb = el(nc.sbuf_tensor(f"sc_ol{S2}", [128, 512], BF16))
                sn.mucol = el(nc.sbuf_tensor(f"sc_mu{S2}", [128, 4], F32))
                sn.vcol = el(nc.sbuf_tensor(f"sc_vc{S2}", [128, 4], F32))
                sets.append(sn)
            pF = [el(nc.psum_tensor(f"sc_pF{L}", [128, 512], F32)) for L in range(4)]
            pDt = [el(nc.psum_tensor(f"sc_pD{L}", [128, 512], BF16)) for L in range(4)]
            id_b = el(nc.sbuf_tensor("id_b", [128, 128], BF16))
            nm_ts4 = el(nc.sbuf_tensor("nm_ts4", [128, 512], BF16))
            nm_st4 = el(nc.sbuf_tensor("nm_st4", [128, 512], BF16))
            qm_st4 = el(nc.sbuf_tensor("qm_st4", [128, 512], BF16))
            rowA = el(nc.sbuf_tensor("sc_rA", [2, 2 * 1024], F32))
            rowB = el(nc.sbuf_tensor("sc_rB", [2, 2 * 1024], F32))
            rowC = el(nc.sbuf_tensor("sc_rC", [2, 2 * 1024], F32))
            T0f = el(nc.sbuf_tensor("sc_T0f", [128, 1024], F32))
            T0b = el(nc.sbuf_tensor("sc_T0b", [128, 1024], BF16))

            with Phase(st, "scanld") as ph:
                @ph.eng("y")
                def _(y):
                    st.inc(y.dma_start(out=id_b[:, :], in_=ident_b.ap()), "sccl", 16)
                    for i4 in range(4):
                        st.inc(y.dma_start(out=nm_ts4[:, i4 * 128:i4 * 128 + 128], in_=nmask_ts.ap()), "sccl", 16)
                        st.inc(y.dma_start(out=nm_st4[:, i4 * 128:i4 * 128 + 128], in_=nmask_st.ap()), "sccl", 16)
                        st.inc(y.dma_start(out=qm_st4[:, i4 * 128:i4 * 128 + 128], in_=qmask_st.ap()), "sccl", 16)
                    y.wait_ge(st.sem("sccl"), st.cnt["sccl"])
                    return None

            with Phase(st, "scan") as ph:
                ch = Chain(st, "sc", nlanes=4, recycle=[
                    "gumm", "guev", "gusg", "gust", "dnev", "dnhl", "dnmm",
                    "dnos", "pjab", "pjao", "pjev", "pjevg", "pjmm", "pjrb",
                    "pjst", "pjxl", "lnmm", "lnsq", "lnst", "lnxh"])
                SQN = ["cvc", "cvk", "cvl", "cvo"]
                SOW = ["cvs", "cvw", "cbl", "cbv"]
                bq = [st.base(SQN[L]) for L in range(4)]
                br = st.base("scqr")
                bow = [st.base(SOW[L]) for L in range(4)]
                brd = st.base("scrd")

                def dma_qkv(y, it):
                    c2, L = divmod(it, 4)
                    sn = sets[L // 2]
                    sl = (c2 % 2) * 512 + (L % 2) * 256
                    for srcd, dstt in ((qkv_d[0], sn.qt), (qkv_d[1], sn.kt), (qkv_d[2], sn.vt)):
                        i = y.dma_start(
                            out=dstt[:, sl:sl + 256].rearrange("p (h t) -> p h t", h=2),
                            in_=srcd.ap().rearrange("(h p) t -> p h t", p=128)
                                [:, L * 2:L * 2 + 2, c2 * 128:c2 * 128 + 128])
                        st.inc(i, SQN[L], 16)

                def dma_rows(y, c2):
                    slr = (c2 % 2) * 1024
                    r0 = c2 * 8
                    if c2 >= 2:
                        y.wait_ge(st.sem("scrd"), brd + 4 * (c2 - 1))
                    for srcd, dstt, row in ((am_d, rowA, 0), (nl_d, rowB, 1), (lc_d, rowC, 0)):
                        i = y.dma_start(
                            out=dstt[row:row + 1, slr:slr + 1024],
                            in_=srcd[r0:r0 + 8, :].rearrange("h t -> (h t)")[None, :])
                        st.inc(i, "scqr", 16)

                def prologue(e):
                    e.memset(T0f[:, :], 0.0)
                    e.memset(T0b[:, :], 0.0)
                    e.memset(rowA[0:2, :], 1.0)
                    e.memset(rowB[0:2, :], 1.0)
                    return e.memset(rowC[0:2, :], 1.0)
                i_pro = ch.add("v", prologue, lane=0)

                def dma0(y):
                    for it0 in range(8):
                        dma_qkv(y, it0)
                    dma_rows(y, 0)
                    dma_rows(y, 1)
                    return y.nop()
                i_dm0 = ch.add("y", dma0, lane=0)
                lane_started = [False] * 4
                lane_steps = [[] for _ in range(4)]

                for it in range(NCHK * 4):
                    c2, L = divmod(it, 4)
                    sn = sets[L // 2]
                    pf = pF[L]
                    pd = pDt[L]
                    hf = L % 2
                    qsl = (c2 % 2) * 512 + hf * 256
                    fsl = hf * 256
                    slr = (c2 % 2) * 1024
                    needq = bq[L] + (c2 + 1) * 3 * 16
                    needr = br + (c2 + 1) * 3 * 16

                    def S(ek, fn, L=L):
                        lane_steps[L].append((ek, fn))

                    def mk(sn=sn, pf=pf, pd=pd, c2=c2, L=L, hf=hf, qsl=qsl, fsl=fsl,
                           slr=slr, needq=needq, needr=needr, it=it):
                        hsl = [slice(fsl + h * 128, fsl + h * 128 + 128) for h in range(2)]
                        f2 = slice(fsl, fsl + 256)
                        ah = [slice(h * 128, h * 128 + 128) for h in range(2)]
                        bh = [slice(256 + h * 128, 256 + h * 128 + 128) for h in range(2)]
                        aF = slice(0, 256)
                        bF = slice(256, 512)

                        def pre_qkv(y):
                            if it + 8 < NCHK * 4:
                                dma_qkv(y, it + 8)
                            return y.nop()

                        def pre_rows(y):
                            if L == 3 and c2 + 2 < NCHK:
                                dma_rows(y, c2 + 2)
                            return y.nop()

                        def t_gkq_om(t):
                            t.wait_ge(st.sem(SQN[L]), needq)
                            t.wait_ge(st.sem("scqr"), needr)
                            for h in range(2):
                                hh = qsl + h * 128
                                t.matmul(pf[:, bh[h]], sn.kt[:, hh:hh + 128], sn.kt[:, hh:hh + 128], start=True, stop=True)
                                rr = slr + (L * 2 + h) * 128
                                i = t.matmul(pf[:, ah[h]], rowA[0:2, rr:rr + 128], rowB[0:2, rr:rr + 128], start=True, stop=True)
                            return i

                        def v_min1(v):
                            return v.tensor_scalar(sn.scr1[:, f2], pf[:, aF], 0.0, None, ALU.min)

                        def s_exp1(s):
                            return s.activation(sn.scr1[:, f2], sn.scr1[:, f2], AF.Exp)

                        def t_omt(t):
                            for h in range(2):
                                rr = slr + (L * 2 + h) * 128
                                i = t.matmul(pf[:, ah[h]], rowB[0:2, rr:rr + 128], rowA[0:2, rr:rr + 128], start=True, stop=True)
                            return i

                        def v_n_min2(v):
                            v.tensor_mul(sn.scr2[:, f2], sn.scr1[:, f2], pf[:, bF])
                            v.tensor_mul(sn.Nb[:, f2], sn.scr2[:, f2], nm_ts4[:, 0:256])
                            return v.tensor_scalar(sn.scr1[:, f2], pf[:, aF], 0.0, None, ALU.min)

                        def t_oq(t):
                            for h in range(2):
                                rr = slr + (L * 2 + h) * 128
                                i = t.matmul(pf[:, ah[h]], rowB[0:2, rr:rr + 128], rowC[0:2, rr:rr + 128], start=True, stop=True)
                            st.inc(t.nop(), "scrd", 1)
                            return i

                        def v_nt_min3(v):
                            v.tensor_mul(sn.scr2[:, f2], sn.scr1[:, f2], pf[:, bF])
                            v.tensor_mul(sn.Ntb[:, f2], sn.scr2[:, f2], nm_st4[:, 0:256])
                            return v.tensor_scalar(sn.scr1[:, f2], pf[:, aF], 0.0, None, ALU.min)

                        def t_kq(t):
                            for h in range(2):
                                hh = qsl + h * 128
                                i = t.matmul(pf[:, ah[h]], sn.kt[:, hh:hh + 128], sn.qt[:, hh:hh + 128], start=True, stop=True)
                            return i

                        def v_sq(v):
                            v.tensor_mul(sn.scr2[:, f2], sn.scr1[:, f2], pf[:, aF])
                            return v.tensor_mul(sn.Sqb[:, f2], sn.scr2[:, f2], qm_st4[:, 0:256])

                        return pre_qkv, pre_rows, t_gkq_om, v_min1, s_exp1, t_omt, v_n_min2, t_oq, v_nt_min3, t_kq, v_sq
                    (pre_qkv, pre_rows, t_gkq_om, v_min1, s_exp1, t_omt, v_n_min2,
                     t_oq, v_nt_min3, t_kq, v_sq) = mk()
                    S("t", t_gkq_om)
                    S("v", v_min1)
                    S("s", s_exp1)
                    S("t", t_omt)
                    S("v", v_n_min2)
                    S("s", s_exp1)
                    S("t", t_oq)
                    S("y", pre_rows)
                    S("v", v_nt_min3)
                    S("s", s_exp1)
                    S("t", t_kq)
                    S("v", v_sq)
                    for k in range(6):
                        def mk_sq2(sn=sn, pf=pf, k=k, fsl=fsl):
                            f2 = slice(fsl, fsl + 256)
                            ah = [slice(h * 128, h * 128 + 128) for h in range(2)]
                            bh = [slice(256 + h * 128, 256 + h * 128 + 128) for h in range(2)]
                            wk = sn.Ntb if k == 0 else sn.Wb[k]
                            tk = sn.Nb if k == 0 else sn.Tb[k % 2]

                            def fmm(t):
                                for h in range(2):
                                    hs = slice(fsl + h * 128, fsl + h * 128 + 128)
                                    i = t.matmul(pf[:, bh[h]], tk[:, hs], wk[:, hs], start=True, stop=True)
                                    if k < 5:
                                        i = t.matmul(pf[:, ah[h]], wk[:, hs], tk[:, hs], start=True, stop=True)
                                return i

                            def fev(s):
                                i = s.copy(sn.Wb[k + 1][:, f2], pf[:, 256:512])
                                if k < 5:
                                    i = s.copy(sn.Tb[(k + 1) % 2][:, f2], pf[:, 0:256])
                                return i
                            return fmm, fev
                        fmm, fev = mk_sq2()
                        S("t", fmm)
                        S("s", fev)

                    def mk2(sn=sn, pf=pf, pd=pd, c2=c2, L=L, hf=hf, qsl=qsl, fsl=fsl, it=it):
                        hsl = [slice(fsl + h * 128, fsl + h * 128 + 128) for h in range(2)]
                        ah = [slice(h * 128, h * 128 + 128) for h in range(2)]
                        col0 = c2 * HPC + L * 2

                        def t_kt0_vt(t):
                            for h in range(2):
                                hh = qsl + h * 128
                                gh = (L * 2 + h) * 128
                                t.matmul(pf[:, ah[h]], sn.kt[:, hh:hh + 128],
                                         T0b[:, gh:gh + 128],
                                         start=True, stop=True)
                                i = t.transpose(pd[:, ah[h]], sn.vt[:, hh:hh + 128], id_b[:, :])
                            return i

                        def v_r(v):
                            for h in range(2):
                                col = col0 + h
                                v.tensor_scalar(sn.scr1[:, hsl[h]], pf[:, ah[h]],
                                                bgp_c[:, col:col + 1], None, ALU.mult)
                                i = v.scalar_tensor_tensor(
                                    sn.Rb[:, hsl[h]], pd[:, ah[h]],
                                    bco_c[:, col:col + 1], sn.scr1[:, hsl[h]],
                                    ALU.mult, ALU.subtract)
                            return i

                        def t_qt0(t):
                            for h in range(2):
                                hh = qsl + h * 128
                                gh = (L * 2 + h) * 128
                                i = t.matmul(pf[:, ah[h]], sn.qt[:, hh:hh + 128],
                                             T0b[:, gh:gh + 128],
                                             start=True, stop=True)
                            return i

                        def v_o1(s):
                            for h in range(2):
                                col = col0 + h
                                i = s.activation(sn.o1[:, hsl[h]], pf[:, ah[h]],
                                                 AF.Copy, scale=gam_c[:, col:col + 1])
                            return i
                        return t_kt0_vt, v_r, t_qt0, v_o1
                    t_kt0_vt, v_r, t_qt0, v_o1 = mk2()
                    S("t", t_kt0_vt)
                    S("v", v_r)
                    S("t", t_qt0)
                    S("s", v_o1)
                    for k in range(7):
                        def mk_ap(sn=sn, pf=pf, k=k, fsl=fsl):
                            f2 = slice(fsl, fsl + 256)
                            hsl = [slice(fsl + h * 128, fsl + h * 128 + 128) for h in range(2)]
                            ah = [slice(h * 128, h * 128 + 128) for h in range(2)]
                            wk = sn.Ntb if k == 0 else sn.Wb[k]
                            src = sn.Rb if k == 0 else sn.ub

                            def fmm(t):
                                for h in range(2):
                                    i = t.matmul(pf[:, ah[h]], wk[:, hsl[h]], src[:, hsl[h]], start=True, stop=True)
                                return i

                            def fev(v):
                                return v.tensor_add(sn.ub[:, f2], pf[:, 0:256], src[:, f2])
                            return fmm, fev
                        fmm, fev = mk_ap()
                        S("t", fmm)
                        S("v", fev)

                    def mk3(sn=sn, pf=pf, pd=pd, c2=c2, L=L, hf=hf, qsl=qsl, fsl=fsl, it=it):
                        hsl = [slice(fsl + h * 128, fsl + h * 128 + 128) for h in range(2)]
                        f2 = slice(fsl, fsl + 256)
                        ah = [slice(h * 128, h * 128 + 128) for h in range(2)]
                        col0 = c2 * HPC + L * 2

                        def t_o2(t):
                            for h in range(2):
                                i = t.matmul(pf[:, ah[h]], sn.Sqb[:, hsl[h]],
                                             sn.ub[:, hsl[h]], start=True, stop=True)
                            return i

                        def v_bn(v):
                            v.tensor_add(sn.Ob[:, f2], sn.o1[:, f2], pf[:, 0:256])
                            for h in range(2):
                                v.reduce_sum(sn.mucol[:, hf * 2 + h:hf * 2 + h + 1], sn.Ob[:, hsl[h]], AX.X)
                            v.tensor_scalar(sn.mucol[:, hf * 2:hf * 2 + 2], sn.mucol[:, hf * 2:hf * 2 + 2], 1.0 / 128.0, None, ALU.mult)
                            for h in range(2):
                                v.tensor_scalar(sn.z0[:, hsl[h]], sn.Ob[:, hsl[h]],
                                                sn.mucol[:, hf * 2 + h:hf * 2 + h + 1], None, ALU.subtract)
                            for h in range(2):
                                i = v.scalar_tensor_tensor(sn.scr2[:, hsl[h]],
                                                           sn.z0[:, hsl[h]], 1.0,
                                                           sn.z0[:, hsl[h]],
                                                           ALU.mult, ALU.mult,
                                                           accum_out=sn.vcol[:, hf * 2 + h:hf * 2 + h + 1])
                            return i

                        def s_rstd(s):
                            s.activation(sn.vcol[:, hf * 2:hf * 2 + 2], sn.vcol[:, hf * 2:hf * 2 + 2], AF.Ln, bias=eps_c[:, 0:1], scale=1.0 / 128.0)
                            return s.activation(sn.vcol[:, hf * 2:hf * 2 + 2], sn.vcol[:, hf * 2:hf * 2 + 2], AF.Exp, scale=-0.5)

                        def v_z(s):
                            for h in range(2):
                                i = s.activation(sn.zb[:, hsl[h]], sn.z0[:, hsl[h]],
                                                 AF.Copy, scale=sn.vcol[:, hf * 2 + h:hf * 2 + h + 1])
                            return i

                        def t_zt(t):
                            for h in range(2):
                                i = t.transpose(pd[:, ah[h]], sn.zb[:, hsl[h]], id_b[:, :])
                            return i

                        def v_oln(v):
                            if c2 >= 1:
                                v.wait_ge(st.sem(SOW[L]), bow[L] + c2 * 16)
                            return v.tensor_scalar(sn.olnb[:, f2], pd[:, 0:256], lnw_sb[:, 0:1],
                                                   lnb_sb[:, 0:1], ALU.mult, ALU.add)

                        def y_oln(y):
                            i = y.dma_start(
                                out=oln_d.ap().rearrange("(h p) t -> p h t", p=128)
                                    [:, L * 2:L * 2 + 2, c2 * 128:c2 * 128 + 128],
                                in_=sn.olnb[:, f2].rearrange("p (h t) -> p h t", h=2))
                            st.inc(i, SOW[L], 16)
                            return y.nop()

                        def t_ktr(t):
                            for h in range(2):
                                hh = qsl + h * 128
                                i = t.transpose(pd[:, ah[h]], sn.kt[:, hh:hh + 128], id_b[:, :])
                            return i

                        def v_ks(s):
                            for h in range(2):
                                col = col0 + h
                                i = s.activation(sn.Ks4[:, hsl[h]], pd[:, ah[h]],
                                                 AF.Copy, scale=gsf_c[:, col:col + 1])
                            return i

                        def t_tu(t):
                            for h in range(2):
                                i = t.matmul(pf[:, ah[h]], sn.Ks4[:, hsl[h]],
                                             sn.ub[:, hsl[h]], start=True, stop=True)
                            return i

                        def v_tu(v):
                            for h in range(2):
                                col = col0 + h
                                gh = (L * 2 + h) * 128
                                v.scalar_tensor_tensor(T0f[:, gh:gh + 128], T0f[:, gh:gh + 128],
                                                       gnd_c[:, col:col + 1],
                                                       pf[:, ah[h]],
                                                       ALU.mult, ALU.add)
                            return v.tensor_copy(T0b[:, L * 256:L * 256 + 256],
                                                 T0f[:, L * 256:L * 256 + 256])
                        return t_o2, v_bn, s_rstd, v_z, t_zt, v_oln, y_oln, t_ktr, v_ks, t_tu, v_tu
                    t_o2, v_bn, s_rstd, v_z, t_zt, v_oln, y_oln, t_ktr, v_ks, t_tu, v_tu = mk3()
                    S("t", t_o2)
                    S("v", v_bn)
                    S("s", s_rstd)
                    S("s", v_z)
                    S("t", t_zt)
                    S("v", v_oln)
                    S("y", y_oln)
                    S("t", t_ktr)
                    S("y", pre_qkv)
                    S("s", v_ks)
                    S("t", t_tu)
                    S("v", v_tu)
                for k in range(max(len(s_) for s_ in lane_steps)):
                    for L2 in range(4):
                        if k < len(lane_steps[L2]):
                            ek, fn = lane_steps[L2][k]
                            dep = None
                            if not lane_started[L2]:
                                dep = [i_pro, i_dm0]
                                lane_started[L2] = True
                            ch.add(ek, fn, lane=L2, after=dep)

                def fin_wait(y):
                    for L2 in range(4):
                        y.wait_ge(st.sem(SOW[L2]), st.cnt[SOW[L2]])
                    return y.nop()
                ch.add("y", fin_wait, lane=0)
                ch.emit(ph)
        with ExitStack() as ls:
            el = ls.enter_context
            wo_sb = el(nc.sbuf_tensor("op_wo", [128, HPC * DM], BF16))
            olt = el(nc.sbuf_tensor("op_ol", [128, 2 * 1024], BF16))
            ggt = el(nc.sbuf_tensor("op_gg", [128, 2 * 1024], BF16))
            og = el(nc.sbuf_tensor("op_og", [128, 2 * 1024], BF16))
            wrow = el(nc.sbuf_tensor("op_wr", [128, 2 * 2048], BF16))
            pso = el(nc.psum_tensor("op_ps", [128, 2048], F32))
            pso2 = el(nc.psum_tensor("op_ps2", [128, 2048], F32))
            with Phase(st, "wo") as ph:
                bw = st.base("wg")
                bl = st.base("wol")
                bgm = st.base("wogm")
                bm = st.base("womm")
                bev = st.base("woev")
                bst = st.base("wost")

                @ph.eng("g")
                def _(g):
                    st.inc(g.dma_start(out=wo_sb[:, :], in_=wo_r.ap()), "wg", 16)
                    bex = st.base("ex2s")
                    for arm in nc.switch_core_id(engines=[g], n=128):
                        pos = _shard_of_arm(int(arm)) & 1
                        opp = 1 - pos
                        hdst = [rsm_d.ap() if pos == 0 else inbox.ap()[opp],
                                inbox.ap()[opp] if pos == 0 else rsm_d.ap()]
                        for half in range(2):
                            g.wait_ge(st.sem("wost"), bst + (half + 1) * 256)
                            g.dma_start(out=hdst[half],
                                        in_=wop_d[half * TC:half * TC + TC, :]).then_inc(st.sem("ex2s"), 16)
                        g.wait_ge(st.sem("ex2s"), bex + 32)
                        g.dma_start(out=flags[pos:pos + 1, 1:2], in_=nonce_sb[0:1, 1:2]).then_inc(st.sem("ex2s"), 16)
                    st.cnt["ex2s"] += 48
                    return None

                @ph.eng("y")
                def _(y):
                    for tt in range(32):
                        sl = (tt % 2) * 1024
                        if tt >= 2:
                            y.wait_ge(st.sem("wogm"), bgm + tt - 1)
                        i = y.dma_start(
                            out=olt[:, sl:sl + 1024].rearrange("p (h t) -> p h t", h=8),
                            in_=oln_d.ap().rearrange("(h p) t -> p h t", p=128)[:, :, tt * 128:tt * 128 + 128])
                        st.inc(i, "wol", 16)
                        i = y.dma_start(
                            out=ggt[:, sl:sl + 1024].rearrange("p (h t) -> p h t", h=8),
                            in_=g_d.ap().rearrange("(h p) t -> p h t", p=128)[:, :, tt * 128:tt * 128 + 128])
                        st.inc(i, "wol", 16)
                        if tt >= 2:
                            y.wait_ge(st.sem("woev"), bev + tt - 1)
                            m = tt - 2
                            i = y.dma_start(out=wop_d[m * 128:(m + 1) * 128, :],
                                            in_=wrow[:, (m % 2) * 2048:(m % 2) * 2048 + 2048])
                            st.inc(i, "wost", 16)
                    for m in (30, 31):
                        y.wait_ge(st.sem("woev"), bev + m + 1)
                        i = y.dma_start(out=wop_d[m * 128:(m + 1) * 128, :],
                                        in_=wrow[:, (m % 2) * 2048:(m % 2) * 2048 + 2048])
                        st.inc(i, "wost", 16)
                    y.wait_ge(st.sem("wost"), st.cnt["wost"])
                    return None

                @ph.eng("v")
                def _(v):
                    for tt in range(32):
                        sl = (tt % 2) * 1024
                        v.wait_ge(st.sem("wol"), bl + tt * 32 + 32)
                        if tt >= 2:
                            v.wait_ge(st.sem("womm"), bm + (tt - 1) * 4)
                        i = v.tensor_mul(og[:, sl:sl + 1024], olt[:, sl:sl + 1024], ggt[:, sl:sl + 1024])
                        st.inc(i, "wogm", 1)
                        v.wait_ge(st.sem("womm"), bm + tt * 4 + 4)
                        if tt >= 2:
                            v.wait_ge(st.sem("wost"), bst + (tt - 1) * 16)
                        pp = pso if tt % 2 == 0 else pso2
                        i = v.tensor_copy(wrow[:, (tt % 2) * 2048:(tt % 2) * 2048 + 2048], pp[:, :])
                        st.inc(i, "woev", 1)
                    return None

                @ph.eng("t")
                def _(t):
                    t.wait_ge(st.sem("wg"), bw + 16)
                    for tt in range(32):
                        sl = (tt % 2) * 1024
                        t.wait_ge(st.sem("wogm"), bgm + tt + 1)
                        if tt >= 2:
                            t.wait_ge(st.sem("woev"), bev + tt - 1)
                        pp = pso if tt % 2 == 0 else pso2
                        for osl in range(4):
                            for h in range(HPC):
                                i = t.matmul(pp[:, osl * 512:osl * 512 + 512],
                                             og[:, sl + h * 128:sl + h * 128 + 128],
                                             wo_sb[:, h * DM + osl * 512:h * DM + osl * 512 + 512],
                                             start=(h == 0), stop=(h == HPC - 1))
                            st.inc(i, "womm", 1)
                    return None

            with Phase(st, "ex2") as ph:
                @ph.eng("g")
                def _(g):
                    g.wait_ge(st.sem("ex2s"), st.cnt["ex2s"])
                    with g.register("dn2") as dn, g.register("exv2") as exv, \
                            g.register("bk2") as bk:
                        g.reg_mov(dn, 0)
                        g.reg_mov(exv, st.cnt["ex2s"])
                        with g.While(lambda: g.snap(dn) == 0):
                            g.dma_start(out=flag_sb[:, :], in_=flags[:, :]).then_inc(st.sem("ex2s"), 16)
                            g.reg_add(exv, exv, 16)
                            g.wait_ge(st.sem("ex2s"), exv)
                            v0 = g.value_load(flag_sb[0:1, 1:2])
                            v1 = g.value_load(flag_sb[1:2, 1:2])
                            vn = g.value_load(nonce_sb[0:1, 1:2])
                            c0 = (g.snap(v0) == g.snap(vn))
                            c1 = (g.snap(v1) == g.snap(vn))
                            g.reg_mov(dn, 1)
                            with g.If(g.snap(c0) == 0):
                                g.reg_mov(dn, 0)
                            with g.If(g.snap(c1) == 0):
                                g.reg_mov(dn, 0)
                            with g.If(g.snap(dn) == 0):
                                for _sp in range(600):
                                    g.reg_add(bk, bk, 1)
                    for arm in nc.switch_core_id(engines=[g], n=128):
                        pos = _shard_of_arm(int(arm)) & 1
                        g.dma_start(out=rsp_d.ap(), in_=inbox.ap()[pos]).then_inc(st.sem("ex2f"), 16)
                    st.cnt.setdefault("ex2f", 0)
                    st.cnt["ex2f"] += 16
                    g.wait_ge(st.sem("ex2f"), st.cnt["ex2f"])
                    return None

            ta = el(nc.sbuf_tensor("cb_a", [128, 2 * 2048], BF16))
            tb_ = el(nc.sbuf_tensor("cb_b", [128, 2 * 2048], BF16))
            with Phase(st, "comb") as ph:
                bl = st.base("cbl")
                bv = st.base("cbv")

                @ph.eng("y")
                def _(y):
                    for fc in range(FC):
                        if fc >= 2:
                            y.wait_ge(st.sem("cbv"), bv + fc - 1)
                        i = y.dma_start_transpose(
                            out=ta[:, (fc % 2) * 2048:(fc % 2) * 2048 + 2048],
                            in_=rsm_d[:, fc * 128:(fc + 1) * 128])
                        st.inc(i, "cbl", 16)
                        i = y.dma_start_transpose(
                            out=tb_[:, (fc % 2) * 2048:(fc % 2) * 2048 + 2048],
                            in_=rsp_d[:, fc * 128:(fc + 1) * 128])
                        st.inc(i, "cbl", 16)
                    return None

                @ph.eng("v")
                def _(v):
                    for fc in range(FC):
                        v.wait_ge(st.sem("cbl"), bl + (fc + 1) * 32)
                        sl = (fc % 2) * 2048
                        v.tensor_add(ta[:, sl:sl + 2048], ta[:, sl:sl + 2048], tb_[:, sl:sl + 2048])
                        i = v.tensor_add(resid[:, fc * TC:fc * TC + TC],
                                         resid[:, fc * TC:fc * TC + TC], ta[:, sl:sl + 2048])
                        st.inc(i, "cbv", 1)
                    return None
        if nstage == 7:
            dump_resid()
            return nc

        if nstage >= 99:
            emit_ffn(final=True, tag="f2")
        else:
            emit_ffn(final=True, tag="f2", ngroups=nstage - 90)
            dump_resid()

    return nc


def _prep_core_inputs(c, inputs, nonce_vals):
    b, g = c >> 1, c & 1
    f32 = np.float32
    x = np.asarray(inputs["L0_attn_x"], f32)
    cols = slice(g * 1024, (g + 1) * 1024)
    hsl = slice(8 * g, 8 * g + 8)
    d = {}
    d["x_fm"] = np.ascontiguousarray(x[b, g * TC:(g + 1) * TC, :].T)
    d["nonce"] = nonce_vals
    gw = np.asarray(inputs["L0_ffn_gate_w"], f32)
    uw = np.asarray(inputs["L0_ffn_up_w"], f32)
    dw = np.asarray(inputs["L0_ffn_down_w"], f32)
    d["gw_r"] = np.ascontiguousarray(gw.T.reshape(FC, 128, NIC, 128).transpose(2, 1, 0, 3).reshape(NIC, 128, DM))
    d["uw_r"] = np.ascontiguousarray(uw.T.reshape(FC, 128, NIC, 128).transpose(2, 1, 0, 3).reshape(NIC, 128, DM))
    d["dw_r"] = np.ascontiguousarray(dw.T.reshape(NIC, 128, DM))
    d["gs_r"] = np.ascontiguousarray(np.asarray(inputs["L0_ffn_gate_s"], f32).reshape(NIC, 128).T)
    d["us_r"] = np.ascontiguousarray(np.asarray(inputs["L0_ffn_up_s"], f32).reshape(NIC, 128).T)
    d["ds_r"] = np.ascontiguousarray(np.asarray(inputs["L0_ffn_down_s"], f32).reshape(FC, 128).T)
    for nm, key in (("wq_r", "L0_attn_Wq"), ("wk_r", "L0_attn_Wk"),
                    ("wv_r", "L0_attn_Wv"), ("wgt_r", "L0_attn_Wg")):
        w = np.asarray(inputs[key], f32)[:, cols]
        d[nm] = np.ascontiguousarray(w.reshape(FC, 128, HPC, 128).transpose(2, 1, 0, 3).reshape(HPC, 128, DM))
    wa = np.asarray(inputs["L0_attn_Wa"], f32)[:, hsl]
    wb = np.asarray(inputs["L0_attn_Wb"], f32)[:, hsl]
    d["wab_r"] = np.ascontiguousarray(np.concatenate([wa, wb], axis=1))
    d["abb_r"] = np.ascontiguousarray(np.concatenate(
        [np.asarray(inputs["L0_attn_ba"], f32)[hsl],
         np.asarray(inputs["L0_attn_bb"], f32)[hsl]])[None, :])
    wo = np.asarray(inputs["L0_attn_Wo"], f32)[cols, :]
    d["wo_r"] = np.ascontiguousarray(wo.reshape(HPC, 128, DM).transpose(1, 0, 2).reshape(128, HPC * DM))
    for nm, key in (("qcw_r", "L0_attn_qcw"), ("kcw_r", "L0_attn_kcw"), ("vcw_r", "L0_attn_vcw")):
        w = np.asarray(inputs[key], f32)[g * 1024:(g + 1) * 1024, :]
        d[nm] = np.ascontiguousarray(w.reshape(HPC, 128, 4).transpose(1, 0, 2).reshape(128, 32))
    for nm, key in (("qcb_r", "L0_attn_qcb"), ("kcb_r", "L0_attn_kcb"), ("vcb_r", "L0_attn_vcb")):
        w = np.asarray(inputs[key], f32)[g * 1024:(g + 1) * 1024]
        d[nm] = np.ascontiguousarray(w.reshape(HPC, 128).T)
    d["lnw_r"] = np.ascontiguousarray(np.asarray(inputs["L0_attn_ln_w"], f32)[:, None])
    d["lnb_r"] = np.ascontiguousarray(np.asarray(inputs["L0_attn_ln_b"], f32)[:, None])
    idm = np.eye(128, dtype=f32)
    d["ident_fr"] = idm
    d["ident_b"] = idm.astype(BFNP)
    tt, ss = np.meshgrid(np.arange(128), np.arange(128), indexing="ij")
    d["tri_inc"] = (tt <= ss).astype(f32)
    d["tri_suf"] = (tt > ss).astype(f32)
    d["nmask_ts"] = (-(ss < tt).astype(f32)).astype(BFNP)
    d["nmask_st"] = (-(ss > tt).astype(f32)).astype(BFNP)
    d["qmask_st"] = (tt <= ss).astype(f32).astype(BFNP)
    return d


def kernel(**inputs):
    nonce_vals = np.random.randint(1, 2 ** 30, size=(1, 4)).astype(np.int32)
    nc = build_program()
    in_maps = [_prep_core_inputs(c, inputs, nonce_vals) for c in range(8)]
    res = run_bass_kernel_spmd(nc, in_maps, core_ids=list(range(8)))
    out = np.zeros((4, T, DM), np.float32)
    for c in range(8):
        b, g = c >> 1, c & 1
        out[b, g * TC:(g + 1) * TC, :] = res.results[c]["out_fm"].T
    return out
